# version 03
# baseline (speedup 1.0000x reference)
import numpy as np
import ml_dtypes
from contextlib import ExitStack
import concourse.bass as bass
import concourse.mybir as mybir
from concourse.bass_utils import run_bass_kernel_spmd

F32 = mybir.dt.float32
BF16 = mybir.dt.bfloat16
I16 = mybir.dt.int16
I32 = mybir.dt.int32
U16 = mybir.dt.uint16
AF = mybir.ActivationFunctionType
ALU = mybir.AluOpType
AX = mybir.AxisListType

B, LQ, D, H, DH = 2, 8192, 1024, 16, 64
T = 128
NT_B = LQ // T
SEG = 2048
NT_S = SEG // T
DI = 3072
EPS_LN = 1e-5
EPS_RMS = float(np.finfo(np.float32).eps)
NCORE = 8


class Buf:
    __slots__ = ("wtoks", "rtoks")

    def __init__(self):
        self.wtoks = {}
        self.rtoks = []


class EM:
    ENG = {"pe": "tensor", "act": "scalar", "dve": "vector", "pool": "gpsimd", "sp": "sync"}

    def __init__(self, nc):
        self.nc = nc
        self.ops = {e: [] for e in self.ENG}
        self.cnt = {}
        for e in self.ENG:
            self.cnt[e] = 0
            self.cnt[e + "_d"] = 0
        self.bufs = {}

    def bf(self, name):
        if name not in self.bufs:
            self.bufs[name] = Buf()
        return self.bufs[name]

    def op(self, eng, fn, reads=(), writes=(), dma=False):
        deps = {}
        for b in reads:
            for s, v in b.wtoks.items():
                deps[s] = max(deps.get(s, 0), v)
        for b in writes:
            for s, v in b.wtoks.items():
                deps[s] = max(deps.get(s, 0), v)
            for s, v in b.rtoks:
                deps[s] = max(deps.get(s, 0), v)
        sname = eng + ("_d" if dma else "")
        self.cnt[sname] += 16 if dma else 1
        tok = (sname, self.cnt[sname])
        self.ops[eng].append((sorted(deps.items()), fn, sname, 16 if dma else 1))
        for b in reads:
            b.rtoks.append(tok)
        for b in writes:
            b.wtoks = {tok[0]: tok[1]}
            b.rtoks = []
        return tok

    def finish(self, final_deps):
        deps = {}
        for s, v in final_deps:
            deps[s] = max(deps.get(s, 0), v)
        self.ops["sp"].append((sorted(deps.items()), None, None, 0))

    def build(self, st):
        sems = {n: st.enter_context(self.nc.semaphore(n)) for n in self.cnt}
        block = st.enter_context(self.nc.Block())

        def make(eng_ops):
            def replay(e):
                for waits, fn, sname, amt in eng_ops:
                    for s, v in waits:
                        e.wait_ge(sems[s], v)
                    if fn is not None:
                        fn(e).then_inc(sems[sname], amt)
            return replay

        for short, attr in self.ENG.items():
            getattr(block, attr)(make(self.ops[short]))


def bc_inner(ap, inner):
    return ap.unsqueeze(2).broadcast_to((ap.shape[0], ap.shape[1], inner))



def build_L1():
    nc = bass.Bass("TRN2", target_bir_lowering=False)
    xh = nc.dram_tensor("x", [LQ, D], F32, kind="ExternalInput").ap()
    wh = nc.dram_tensor("w", [256, D], F32, kind="ExternalInput").ap()
    knwh = nc.dram_tensor("knw", [1, DH], F32, kind="ExternalInput").ap()
    lamh = nc.dram_tensor("lam4", [1, 4], F32, kind="ExternalInput").ap()
    etah = nc.dram_tensor("eta4", [1, 4], F32, kind="ExternalInput").ap()
    wgh = nc.dram_tensor("wg", [384, D], F32, kind="ExternalInput").ap()
    wuh = nc.dram_tensor("wu", [384, D], F32, kind="ExternalInput").ap()
    wdh = nc.dram_tensor("wd", [128, DI], F32, kind="ExternalInput").ap()
    oh = nc.dram_tensor("od", [LQ, 256], F32, kind="ExternalOutput").ap()
    wgth = nc.dram_tensor("wgt", [D, 384], BF16, kind="ExternalOutput").ap()
    wuth = nc.dram_tensor("wut", [D, 384], BF16, kind="ExternalOutput").ap()
    wdth = nc.dram_tensor("wdt", [DI, 128], BF16, kind="ExternalOutput").ap()

    em = EM(nc)
    st = ExitStack()
    with st:
        e = st.enter_context
        sb = lambda n, s, d: e(nc.sbuf_tensor(n, s, d))
        kst = sb("kst", [128, NT_B * 256], BF16)
        vst = sb("vst", [128, NT_B * 256], BF16)
        kt01 = sb("kt01", [128, LQ], BF16)
        kt23 = sb("kt23", [128, LQ], BF16)
        wtsl = sb("wtsl", [128, 8 * 256], BF16)
        ones1 = sb("ones1", [1, 128], F32)
        idbf = sb("idbf", [128, 128], BF16)
        onesb = sb("onesb", [128, 128], BF16)
        onesf = sb("onesf", [128, 128], F32)
        idf = sb("idf", [128, 128], F32)
        par = sb("par", [128, 4], F32)
        peta = sb("peta", [128, 4], F32)
        plnl = sb("plnl", [128, 4], F32)
        pbeta = sb("pbeta", [128, 4], F32)
        pnbeta = sb("pnbeta", [128, 4], F32)
        pLam = sb("pLam", [128, 4], F32)
        pnLam = sb("pnLam", [128, 4], F32)
        pWv = sb("pWv", [128, 4], F32)
        plamT = sb("plamT", [128, 4], F32)
        iot1 = sb("iot1", [128, 1], F32)
        iot2 = sb("iot2", [128, 1], F32)
        blraw = sb("blraw", [128, 128], F32)
        buraw = sb("buraw", [128, 128], F32)
        maskL = [sb(f"mL{p}", [128, 128], F32) for p in range(4)]
        maskU = [sb(f"mU{p}", [128, 128], F32) for p in range(4)]
        wkf = sb("wkf", [128, 256], F32)
        xt = [sb(f"xt{i}", [128, D], F32) for i in range(2)]
        hbf = [sb(f"hbf{i}", [128, D], BF16) for i in range(2)]
        hfm = [sb(f"hfm{i}", [128, D], BF16) for i in range(2)]
        xsq = [sb(f"xsq{i}", [128, D], F32) for i in range(2)]
        stat = [sb(f"st{i}", [128, 8], F32) for i in range(2)]
        kms = [sb(f"kms{i}", [128, 8], F32) for i in range(2)]
        ktmp = [sb(f"ktmp{i}", [128, 256], F32) for i in range(2)]
        otile = [sb(f"ot{i}", [128, 256], F32) for i in range(2)]
        Rb = [sb(f"Rb{i}", [128, 64], BF16) for i in range(2)]
        P0b = [sb(f"P0{i}", [128, 128], BF16) for i in range(2)]
        P0Tb = [sb(f"P0T{i}", [128, 128], BF16) for i in range(2)]
        Qb = [[sb(f"Qb{i}{j}", [128, 128], BF16) for j in range(2)] for i in range(2)]
        QTb = [[sb(f"QTb{i}{j}", [128, 128], BF16) for j in range(2)] for i in range(2)]
        Xb = [sb(f"Xb{i}", [128, 64], BF16) for i in range(2)]
        kkv = [sb(f"kkv{i}", [128, 2], F32) for i in range(2)]
        ksq = [sb(f"ksq{i}", [128, 64], F32) for i in range(2)]
        wkb = [sb(f"wkb{i}", [128, 64], BF16) for i in range(2)]
        Sf2 = [[sb(f"Sf{g}{j}", [128, 64], F32) for j in range(2)] for g in range(2)]
        Sbf2 = [[sb(f"Sb{g}{j}", [128, 64], BF16) for j in range(2)] for g in range(2)]
        psG = [e(nc.psum_tensor(f"psG{i}", [128, 128], F32)) for i in range(2)]
        psQ = [e(nc.psum_tensor(f"psQ{i}", [128, 128], F32)) for i in range(2)]
        psQT = [e(nc.psum_tensor(f"psQT{i}", [128, 128], F32)) for i in range(2)]
        psX = [e(nc.psum_tensor(f"psX{i}", [128, 64], F32)) for i in range(2)]
        dqw = [sb(f"dqw{i}", [128, DI], F32) for i in range(1)]
        dqi = [sb(f"dqi{i}", [128, DI], I32) for i in range(1)]
        dqf = [sb(f"dqf{i}", [128, DI], F32) for i in range(1)]
        dqs = [sb(f"dqs{i}", [128, 96], F32) for i in range(1)]
        dqr = [sb(f"dqr{i}", [128, 96], F32) for i in range(1)]
        dqtr = [sb(f"dqtr{i}", [128, 128], BF16) for i in range(1)]
        wtf = sb("wtf", [128, D], F32)
        epsl = sb("epsl", [128, 1], F32)
        epsr = sb("epsr", [128, 1], F32)
        mh = sb("mh", [128, 1], F32)

        O, bf = em.op, em.bf
        c = bf("const")
        O("dve", lambda e: e.memset(epsl[:], EPS_LN), writes=[c])
        O("dve", lambda e: e.memset(epsr[:], EPS_RMS), writes=[c])
        O("dve", lambda e: e.memset(mh[:], -0.5), writes=[c])
        O("dve", lambda e: e.memset(ones1[:], 1.0), writes=[c])
        O("dve", lambda e: e.memset(onesb[:], 1.0), writes=[c])
        O("pool", lambda e: e.iota(blraw[:], [[-1, 128]], base=0, channel_multiplier=1,
                                   allow_small_or_imprecise_dtypes=True), writes=[c])
        O("pool", lambda e: e.iota(buraw[:], [[1, 128]], base=0, channel_multiplier=-1,
                                   allow_small_or_imprecise_dtypes=True), writes=[c])
        O("pool", lambda e: e.iota(iot1[:], [[0, 1]], base=1, channel_multiplier=1,
                                   allow_small_or_imprecise_dtypes=True), writes=[c])
        O("pool", lambda e: e.iota(iot2[:], [[0, 1]], base=127, channel_multiplier=-1,
                                   allow_small_or_imprecise_dtypes=True), writes=[c])
        O("pool", lambda e: e.affine_select(idbf[:], onesb[:], [[-1, 128]], ALU.is_equal, 0.0,
                                            base=0, channel_multiplier=1), reads=[c], writes=[c])
        O("dve", lambda e: e.memset(onesf[:], 1.0), writes=[c])
        O("pool", lambda e: e.affine_select(idf[:], onesf[:], [[-1, 128]], ALU.is_equal, 0.0,
                                            base=0, channel_multiplier=1), reads=[c], writes=[c])
        O("sp", lambda e: e.dma_start(out=par[0:1, :], in_=lamh), writes=[c], dma=True)
        O("sp", lambda e: e.dma_start(out=peta[0:1, :], in_=etah), writes=[c], dma=True)
        O("sp", lambda e: e.dma_start(out=wkf[0:1, 0:DH], in_=knwh), writes=[c], dma=True)

        pq0b = bf("psQ0")

        def brow(dst, n):
            for c0 in range(0, n, 128):
                w = min(128, n - c0)
                O("pe", lambda e, dst=dst, c0=c0, w=w: e.matmul(psQ[0][:, 0:w], ones1[:, :],
                                                                dst[0:1, c0:c0 + w],
                                                                start=True, stop=True),
                  reads=[c], writes=[pq0b])
                O("dve", lambda e, dst=dst, c0=c0, w=w: e.tensor_copy(dst[:, c0:c0 + w], psQ[0][:, 0:w]),
                  reads=[pq0b], writes=[c])

        brow(par, 4)
        brow(peta, 4)
        for r in range(1, 4):
            O("dve", lambda e, r=r: e.tensor_copy(wkf[0:1, r * DH:(r + 1) * DH], wkf[0:1, 0:DH]),
              reads=[c], writes=[c])
        brow(wkf, 256)
        O("act", lambda e: e.activation(par[:], par[:], AF.Sigmoid), reads=[c], writes=[c])
        O("act", lambda e: e.activation(peta[:], peta[:], AF.Tanh), reads=[c], writes=[c])
        O("act", lambda e: e.activation(plnl[:], par[:], AF.Ln), reads=[c], writes=[c])
        O("dve", lambda e: e.tensor_scalar(pbeta[:], par[:], -1.0, 1.0, ALU.mult, ALU.add),
          reads=[c], writes=[c])
        O("dve", lambda e: e.tensor_tensor(pbeta[:], peta[:], pbeta[:], ALU.mult), reads=[c], writes=[c])
        O("dve", lambda e: e.tensor_scalar(pnbeta[:], pbeta[:], 0.015625, None, ALU.mult),
          reads=[c], writes=[c])
        O("dve", lambda e: e.tensor_scalar(pbeta[:], pbeta[:], -0.015625, None, ALU.mult),
          reads=[c], writes=[c])
        for p in range(4):
            O("dve", lambda e, p=p: e.tensor_tensor(pLam[:, p:p + 1], iot1[:], plnl[:, p:p + 1], ALU.mult),
              reads=[c], writes=[c])
            O("act", lambda e, p=p: e.activation(pLam[:, p:p + 1], pLam[:, p:p + 1], AF.Exp),
              reads=[c], writes=[c])
            O("dve", lambda e, p=p: e.tensor_tensor(pWv[:, p:p + 1], iot2[:], plnl[:, p:p + 1], ALU.mult),
              reads=[c], writes=[c])
            O("act", lambda e, p=p: e.activation(pWv[:, p:p + 1], pWv[:, p:p + 1], AF.Exp),
              reads=[c], writes=[c])
            O("dve", lambda e, p=p: e.tensor_tensor(pWv[:, p:p + 1], pWv[:, p:p + 1], pbeta[:, p:p + 1], ALU.mult),
              reads=[c], writes=[c])
            O("dve", lambda e, p=p: e.tensor_scalar(plamT[:, p:p + 1], plnl[:, p:p + 1], 128.0, None, ALU.mult),
              reads=[c], writes=[c])
            O("act", lambda e, p=p: e.activation(plamT[:, p:p + 1], plamT[:, p:p + 1], AF.Exp),
              reads=[c], writes=[c])
            O("dve", lambda e, p=p: e.tensor_scalar(pnLam[:, p:p + 1], pLam[:, p:p + 1], -1.0, None, ALU.mult),
              reads=[c], writes=[c])
            for mk, braw in ((maskL[p], blraw), (maskU[p], buraw)):
                O("dve", lambda e, mk=mk, braw=braw, p=p: e.tensor_scalar(
                    mk[:], braw[:], plnl[:, p:p + 1], None, ALU.mult), reads=[c], writes=[c])
                O("act", lambda e, mk=mk: e.activation(mk[:], mk[:], AF.Exp), reads=[c], writes=[c])
                O("dve", lambda e, mk=mk, p=p: e.tensor_scalar(
                    mk[:], mk[:], pnbeta[:, p:p + 1], None, ALU.mult), reads=[c], writes=[c])
            O("pool", lambda e, p=p: e.affine_select(maskL[p][:], maskL[p][:], [[-1, 128]], ALU.is_ge, 0.0,
                                                     base=-1, channel_multiplier=1), reads=[c], writes=[c])
            O("pool", lambda e, p=p: e.affine_select(maskU[p][:], maskU[p][:], [[1, 128]], ALU.is_ge, 0.0,
                                                     base=-1, channel_multiplier=-1), reads=[c], writes=[c])
        for g in range(2):
            for j in range(2):
                O("dve", lambda e, g=g, j=j: e.memset(Sf2[g][j][:], 0.0),
                  writes=[bf(f"S{2 * g}"), bf(f"S{2 * g + 1}")])
                O("dve", lambda e, g=g, j=j: e.memset(Sbf2[g][j][:], 0.0),
                  writes=[bf(f"S{2 * g}"), bf(f"S{2 * g + 1}")])
        wldb = bf("wload")
        wtslb = bf("wtsl")
        ptrb = bf("psQ0")
        for r in range(2):
            O("sp", lambda e, r=r: e.dma_start(out=wtf[:], in_=wh[128 * r:128 * (r + 1), :]),
              writes=[wldb], dma=True)
            for ci in range(8):
                O("pe", lambda e, ci=ci: e.transpose(psQ[0][:], wtf[:, 128 * ci:128 * (ci + 1)], idf[:]),
                  reads=[wldb, c], writes=[ptrb])
                O("act", lambda e, ci=ci, r=r: e.copy(wtsl[:, 256 * ci + 128 * r:256 * ci + 128 * r + 128],
                                                      psQ[0][:]),
                  reads=[ptrb], writes=[wtslb])

        dq_jobs = []
        for wsrc, wdst, nrows, ncolw, nb in ((wgh, wgth, 384, D, 32), (wuh, wuth, 384, D, 32),
                                             (wdh, wdth, 128, DI, 96)):
            for r0 in range(0, nrows, 128):
                dq_jobs.append((wsrc, wdst, r0, ncolw, nb))

        ost = []
        kvb = bf("kvst")
        ktb_ = bf("ktst")
        for cidx in range(NT_B):
            i = cidx % 2
            xb_ = bf(f"xt{i}"); hb = bf(f"hbf{i}"); hf = bf(f"hfm{i}")
            sq = bf(f"xsq{i}"); stt_ = bf(f"st{i}"); km = bf(f"kms{i}"); ktm = bf(f"ktmp{i}")
            ot = bf(f"otile{i}")
            O("sp", lambda e, cidx=cidx, i=i: e.dma_start(out=xt[i][:], in_=xh[T * cidx:T * (cidx + 1), :]),
              writes=[xb_], dma=True)
            O("act", lambda e, i=i: e.activation(xsq[i][:], xt[i][:], AF.Square), reads=[xb_], writes=[sq])
            O("dve", lambda e, i=i: e.tensor_reduce(stat[i][:, 0:1], xt[i][:], AX.X, ALU.add),
              reads=[xb_], writes=[stt_])
            O("dve", lambda e, i=i: e.tensor_reduce(stat[i][:, 1:2], xsq[i][:], AX.X, ALU.add),
              reads=[sq], writes=[stt_])
            O("dve", lambda e, i=i: e.tensor_scalar(stat[i][:, 2:3], stat[i][:, 0:1], 1.0 / D, None, ALU.mult),
              reads=[stt_], writes=[stt_])
            O("dve", lambda e, i=i: e.tensor_tensor(stat[i][:, 3:4], stat[i][:, 2:3], stat[i][:, 2:3], ALU.mult),
              reads=[stt_], writes=[stt_])
            O("dve", lambda e, i=i: e.tensor_scalar(stat[i][:, 4:5], stat[i][:, 1:2], 1.0 / D, None, ALU.mult),
              reads=[stt_], writes=[stt_])
            O("dve", lambda e, i=i: e.tensor_tensor(stat[i][:, 3:4], stat[i][:, 4:5], stat[i][:, 3:4], ALU.subtract),
              reads=[stt_], writes=[stt_])
            O("act", lambda e, i=i: e.activation(stat[i][:, 5:6], stat[i][:, 3:4], AF.Ln, bias=epsl[:, 0:1]),
              reads=[stt_, c], writes=[stt_])
            O("act", lambda e, i=i: e.activation(stat[i][:, 5:6], stat[i][:, 5:6], AF.Exp, scale=mh[:, 0:1]),
              reads=[stt_, c], writes=[stt_])
            O("dve", lambda e, i=i: e.tensor_tensor(stat[i][:, 6:7], stat[i][:, 2:3], stat[i][:, 5:6], ALU.mult),
              reads=[stt_], writes=[stt_])
            O("dve", lambda e, i=i: e.tensor_scalar(stat[i][:, 6:7], stat[i][:, 6:7], -1.0, None, ALU.mult),
              reads=[stt_], writes=[stt_])
            O("dve", lambda e, i=i: e.tensor_scalar(hbf[i][:], xt[i][:], stat[i][:, 5:6],
                                                    stat[i][:, 6:7], ALU.mult, ALU.add),
              reads=[xb_, stt_], writes=[hb])
            for ck in range(8):
                q = "sp" if ck % 2 == 0 else "act"
                O(q, lambda e, ck=ck, i=i: e.dma_start(out=hfm[i][:, 128 * ck:128 * (ck + 1)],
                                                       in_=hbf[i][:, 128 * ck:128 * (ck + 1)], transpose=True),
                  reads=[hb], writes=[hf], dma=True)
            for half in range(2):
                gb_ = bf(f"psG{half}")
                for ck in range(8):
                    O("pe", lambda e, ck=ck, i=i, half=half: e.matmul(
                        psG[half][:], hfm[i][:, 128 * ck:128 * (ck + 1)],
                        wtsl[:, 256 * ck + 128 * half:256 * ck + 128 * half + 128],
                        start=(ck == 0), stop=(ck == 7)),
                      reads=[hf, wtslb], writes=[gb_])
                O("dve", lambda e, cidx=cidx, half=half: e.tensor_scalar(
                    vst[:, 256 * cidx + 128 * half:256 * cidx + 128 * half + 128], psG[half][:],
                    -1.0, None, ALU.mult),
                  reads=[gb_], writes=[kvb])
            O("act", lambda e, i=i: e.activation(ktmp[i][:], hbf[i][:, 0:256], AF.Square), reads=[hb], writes=[ktm])
            O("dve", lambda e, i=i: e.tensor_reduce(kms[i][:, 0:4],
                                                    ktmp[i][:].rearrange("p (a b) -> p a b", a=4),
                                                    AX.X, ALU.add), reads=[ktm], writes=[km])
            O("dve", lambda e, i=i: e.tensor_scalar(kms[i][:, 0:4], kms[i][:, 0:4], 1.0 / DH, None, ALU.mult),
              reads=[km], writes=[km])
            O("act", lambda e, i=i: e.activation(kms[i][:, 4:8], kms[i][:, 0:4], AF.Ln, bias=epsr[:, 0:1]),
              reads=[km, c], writes=[km])
            O("act", lambda e, i=i: e.activation(kms[i][:, 4:8], kms[i][:, 4:8], AF.Exp, scale=mh[:, 0:1]),
              reads=[km, c], writes=[km])
            O("dve", lambda e, i=i: e.tensor_tensor(ktmp[i][:].rearrange("p (a b) -> p a b", a=4),
                                                    hbf[i][:, 0:256].rearrange("p (a b) -> p a b", a=4),
                                                    bc_inner(kms[i][:, 4:8], DH), ALU.mult),
              reads=[hb, km], writes=[ktm])
            O("dve", lambda e, cidx=cidx, i=i: e.tensor_tensor(kst[:, 256 * cidx:256 * (cidx + 1)], ktmp[i][:],
                                                               wkf[:], ALU.mult),
              reads=[ktm, c], writes=[kvb])
            O("sp", lambda e, cidx=cidx: e.dma_start(out=kt01[:, 128 * cidx:128 * (cidx + 1)],
                                                     in_=kst[:, 256 * cidx:256 * cidx + 128], transpose=True),
              reads=[kvb], writes=[ktb_], dma=True)
            O("act", lambda e, cidx=cidx: e.dma_start(out=kt23[:, 128 * cidx:128 * (cidx + 1)],
                                                      in_=kst[:, 256 * cidx + 128:256 * cidx + 256], transpose=True),
              reads=[kvb], writes=[ktb_], dma=True)

            for p in range(4):
                pi = p % 2
                ktsrc = kt01 if p < 2 else kt23
                pbase = 64 * (p % 2)
                Sp = bf(f"S{p}")
                j0, j1 = cidx % 2, (cidx + 1) % 2
                gpb = bf(f"psG{pi}"); ksb = bf(f"psX{pi}"); xpb = bf(f"psX{pi}")
                qpb = bf(f"psQ{pi}"); qtpb = bf(f"psQT{pi}"); upb = bf(f"psG{pi}")
                rb_ = bf(f"Rb{pi}"); p0b = bf(f"P0{pi}"); p0tb = bf(f"P0T{pi}")
                xbb = bf(f"Xb{pi}"); kkb = bf(f"kk{pi}"); wkbb = bf(f"wkb{pi}"); ksqb = bf(f"ksq{pi}")
                Kt_ap = ktsrc[pbase:pbase + 64, 128 * cidx:128 * (cidx + 1)]
                Ko_ap = kst[:, 256 * cidx + 64 * p:256 * cidx + 64 * (p + 1)]
                Vo_ap = vst[:, 256 * cidx + 64 * p:256 * cidx + 64 * (p + 1)]
                O("pe", lambda e, Kt_ap=Kt_ap, p=p, j0=j0, pbase=pbase, pi=pi: e.matmul(
                    psX[pi][:], Kt_ap, Sbf2[p // 2][j0][pbase:pbase + 64, :], start=True, stop=True),
                  reads=[ktb_, Sp], writes=[ksb])
                O("dve", lambda e, Vo_ap=Vo_ap, p=p, pi=pi: e.scalar_tensor_tensor(
                    Rb[pi][:], psX[pi][:], pnLam[:, p:p + 1], Vo_ap, ALU.mult, ALU.add),
                  reads=[ksb, kvb, c], writes=[rb_])
                O("pe", lambda e, Kt_ap=Kt_ap, pi=pi: e.matmul(psG[pi][:], Kt_ap, Kt_ap, start=True, stop=True),
                  reads=[ktb_], writes=[gpb])
                O("dve", lambda e, p=p, pi=pi: e.tensor_tensor(P0b[pi][:], psG[pi][:], maskL[p][:], ALU.mult),
                  reads=[gpb, c], writes=[p0b])
                O("dve", lambda e, p=p, pi=pi: e.tensor_tensor(P0Tb[pi][:], psG[pi][:], maskU[p][:], ALU.mult),
                  reads=[gpb, c], writes=[p0tb])
                O("pe", lambda e, pi=pi: e.matmul(psX[pi][:], idbf[:], Rb[pi][:], start=True, stop=False),
                  reads=[rb_, c], writes=[xpb])
                O("pe", lambda e, pi=pi: e.matmul(psX[pi][:], P0Tb[pi][:], Rb[pi][:], start=False, stop=False),
                  reads=[rb_, p0tb], writes=[xpb])
                O("act", lambda e, pi=pi: e.copy(Xb[pi][:], psX[pi][:]), reads=[xpb], writes=[xbb])
                qprev, qtprev = P0b[pi], P0Tb[pi]
                qprev_b, qtprev_b = p0b, p0tb
                for m in range(1, 7):
                    jj = m % 2
                    qcur, qtcur = Qb[pi][jj], QTb[pi][jj]
                    qcb, qtcb = bf(f"Q{pi}{jj}"), bf(f"QT{pi}{jj}")
                    if m < 6:
                        O("pe", lambda e, qp=qprev, qtp=qtprev, pi=pi: e.matmul(psQ[pi][:], qtp[:], qp[:],
                                                                        start=True, stop=True),
                          reads=[qprev_b, qtprev_b], writes=[qpb])
                        O("act", lambda e, qc=qcur, pi=pi: e.copy(qc[:], psQ[pi][:]), reads=[qpb], writes=[qcb])
                    O("pe", lambda e, qp=qprev, qtp=qtprev, pi=pi: e.matmul(psQT[pi][:], qp[:], qtp[:],
                                                                    start=True, stop=True),
                      reads=[qprev_b, qtprev_b], writes=[qtpb])
                    O("act", lambda e, qtc=qtcur, pi=pi: e.copy(qtc[:], psQT[pi][:]), reads=[qtpb], writes=[qtcb])
                    O("pe", lambda e, qtc=qtcur, pi=pi, m=m: e.matmul(psX[pi][:], qtc[:], Xb[pi][:],
                                                                     start=False, stop=(m == 6)),
                      reads=[qtcb, xbb], writes=[xpb])
                    O("act", lambda e, pi=pi: e.copy(Xb[pi][:], psX[pi][:]), reads=[xpb], writes=[xbb])
                    qprev, qtprev = qcur, qtcur
                    qprev_b, qtprev_b = qcb, qtcb
                O("act", lambda e, Ko_ap=Ko_ap, pi=pi: e.activation(ksq[pi][:], Ko_ap, AF.Square),
                  reads=[kvb], writes=[ksqb])
                O("dve", lambda e, pi=pi: e.tensor_reduce(kkv[pi][:, 0:1], ksq[pi][:], AX.X, ALU.add),
                  reads=[ksqb], writes=[kkb])
                O("dve", lambda e, p=p, pi=pi: e.tensor_scalar(kkv[pi][:, 1:2], kkv[pi][:, 0:1],
                                                               pbeta[:, p:p + 1], -1.0, ALU.mult, ALU.add),
                  reads=[kkb, c], writes=[kkb])
                O("dve", lambda e, p=p, pi=pi, i=i: e.tensor_scalar(otile[i][:, 64 * p:64 * (p + 1)],
                                                                    psX[pi][:], kkv[pi][:, 1:2], None, ALU.mult),
                  reads=[xpb, kkb], writes=[ot])
                O("dve", lambda e, Ko_ap=Ko_ap, p=p, pi=pi: e.tensor_scalar(wkb[pi][:], Ko_ap,
                                                                            pWv[:, p:p + 1], None, ALU.mult),
                  reads=[kvb, c], writes=[wkbb])
                O("pe", lambda e, pi=pi, pbase=pbase: e.matmul(psG[pi][pbase:pbase + 64, 0:64],
                                                               wkb[pi][:], Xb[pi][:],
                                                               start=True, stop=True,
                                                               tile_position=(0, pbase)),
                  reads=[wkbb, xbb], writes=[upb])
                O("dve", lambda e, p=p, j0=j0, j1=j1, pbase=pbase, pi=pi: e.scalar_tensor_tensor(
                    Sf2[p // 2][j1][pbase:pbase + 64, :], Sf2[p // 2][j0][pbase:pbase + 64, :],
                    plamT[pbase:pbase + 128 - pbase, p:p + 1][0:64, :] if False else plamT[pbase:pbase + 64, p:p + 1],
                    psG[pi][pbase:pbase + 64, 0:64], ALU.mult, ALU.add),
                  reads=[upb, Sp, c], writes=[Sp])
                O("act", lambda e, p=p, j1=j1, pbase=pbase: e.copy(Sbf2[p // 2][j1][pbase:pbase + 64, :],
                                                                   Sf2[p // 2][j1][pbase:pbase + 64, :]),
                  reads=[Sp], writes=[Sp])
            tk = O("pool", lambda e, cidx=cidx, i=i: e.dma_start(out=oh[T * cidx:T * (cidx + 1), :],
                                                                 in_=otile[i][:]),
                   reads=[ot], writes=[ot], dma=True)
            ost.append(tk)

            if cidx % 9 == 4 and dq_jobs:
                wsrc, wdst, r0, ncolw, nb = dq_jobs.pop(0)
                di = 0
                db = bf(f"dq{di}")
                O("sp", lambda e, wsrc=wsrc, r0=r0, ncolw=ncolw, di=di: e.dma_start(
                    out=dqw[di][:, 0:ncolw], in_=wsrc[r0:r0 + 128, :]), writes=[db], dma=True)
                O("dve", lambda e, ncolw=ncolw, nb=nb, di=di: e.tensor_reduce(
                    dqs[di][:, 0:nb], dqw[di][:, 0:ncolw].rearrange("p (a b) -> p a b", b=32),
                    AX.X, ALU.max, apply_absolute_value=True), reads=[db], writes=[db])
                O("dve", lambda e, nb=nb, di=di: e.tensor_scalar(dqs[di][:, 0:nb], dqs[di][:, 0:nb],
                                                                 1.0 / 7.0, 1e-12, ALU.mult, ALU.max),
                  reads=[db], writes=[db])
                O("dve", lambda e, nb=nb, di=di: e.reciprocal(dqr[di][:, 0:nb], dqs[di][:, 0:nb]),
                  reads=[db], writes=[db])
                O("dve", lambda e, ncolw=ncolw, nb=nb, di=di: e.tensor_tensor(
                    dqi[di][:, 0:ncolw].rearrange("p (a b) -> p a b", b=32),
                    dqw[di][:, 0:ncolw].rearrange("p (a b) -> p a b", b=32),
                    bc_inner(dqr[di][:, 0:nb], 32), ALU.mult), reads=[db], writes=[db])
                O("dve", lambda e, ncolw=ncolw, di=di: e.tensor_scalar(dqi[di][:, 0:ncolw], dqi[di][:, 0:ncolw],
                                                                       7, -8, ALU.min, ALU.max),
                  reads=[db], writes=[db])
                O("dve", lambda e, ncolw=ncolw, nb=nb, di=di: e.tensor_tensor(
                    dqf[di][:, 0:ncolw].rearrange("p (a b) -> p a b", b=32),
                    dqi[di][:, 0:ncolw].rearrange("p (a b) -> p a b", b=32),
                    bc_inner(dqs[di][:, 0:nb], 32), ALU.mult), reads=[db], writes=[db])
                for ck in range(ncolw // 128):
                    O("pe", lambda e, ck=ck, di=di: e.transpose(psQ[0][:], dqf[di][:, 128 * ck:128 * (ck + 1)],
                                                                idf[:]),
                      reads=[db, c], writes=[ptrb])
                    O("act", lambda e, di=di: e.copy(dqtr[di][:], psQ[0][:]), reads=[ptrb], writes=[db])
                    tk2 = O("pool", lambda e, wdst=wdst, ck=ck, r0=r0, di=di: e.dma_start(
                        out=wdst[128 * ck:128 * (ck + 1), r0:r0 + 128], in_=dqtr[di][:]),
                        reads=[db], writes=[db], dma=True)
                    ost.append(tk2)
        assert not dq_jobs
        em.finish(ost)
        em.build(st)
    return nc



def ln_stats(O, xtile, sqtile, stattile, xb_, sq, stt_, epsap, mh, cb):
    n = xtile.shape[1]
    O("act", lambda e, xtile=xtile, sqtile=sqtile: e.activation(sqtile[:], xtile[:], AF.Square),
      reads=[xb_], writes=[sq])
    O("dve", lambda e, xtile=xtile, stattile=stattile: e.tensor_reduce(stattile[:, 0:1], xtile[:], AX.X, ALU.add),
      reads=[xb_], writes=[stt_])
    O("dve", lambda e, sqtile=sqtile, stattile=stattile: e.tensor_reduce(stattile[:, 1:2], sqtile[:], AX.X, ALU.add),
      reads=[sq], writes=[stt_])
    O("dve", lambda e, stattile=stattile, n=n: e.tensor_scalar(stattile[:, 2:3], stattile[:, 0:1],
                                                               1.0 / n, None, ALU.mult),
      reads=[stt_], writes=[stt_])
    O("dve", lambda e, stattile=stattile: e.tensor_tensor(stattile[:, 3:4], stattile[:, 2:3],
                                                          stattile[:, 2:3], ALU.mult),
      reads=[stt_], writes=[stt_])
    O("dve", lambda e, stattile=stattile, n=n: e.tensor_scalar(stattile[:, 4:5], stattile[:, 1:2],
                                                               1.0 / n, None, ALU.mult),
      reads=[stt_], writes=[stt_])
    O("dve", lambda e, stattile=stattile: e.tensor_tensor(stattile[:, 3:4], stattile[:, 4:5],
                                                          stattile[:, 3:4], ALU.subtract),
      reads=[stt_], writes=[stt_])
    O("act", lambda e, stattile=stattile, epsap=epsap: e.activation(stattile[:, 5:6], stattile[:, 3:4],
                                                                AF.Ln, bias=epsap),
      reads=[stt_, cb], writes=[stt_])
    O("act", lambda e, stattile=stattile, mh=mh: e.activation(stattile[:, 5:6], stattile[:, 5:6], AF.Exp, scale=mh),
      reads=[stt_, cb], writes=[stt_])
    O("dve", lambda e, stattile=stattile: e.tensor_tensor(stattile[:, 6:7], stattile[:, 2:3],
                                                          stattile[:, 5:6], ALU.mult),
      reads=[stt_], writes=[stt_])
    O("dve", lambda e, stattile=stattile: e.tensor_scalar(stattile[:, 6:7], stattile[:, 6:7],
                                                          -1.0, None, ALU.mult),
      reads=[stt_], writes=[stt_])


def max_chain(O, mc, mcb, idf, psT, psTb, mout, moutb, ncols, extra=None, extrab=None):
    O("dve", lambda e: e.tensor_reduce(mc[:, ncols:ncols + 1], mc[:, 0:ncols], AX.X, ALU.max),
      reads=[mcb], writes=[mcb])
    O("pe", lambda e: e.transpose(psT[0:1, 0:128], mc[:, ncols:ncols + 1], idf[:]),
      reads=[mcb], writes=[psTb])
    O("dve", lambda e: e.tensor_reduce(mout[0:1, 0:1], psT[0:1, 0:128], AX.X, ALU.max),
      reads=[psTb], writes=[moutb])
    if extra is not None:
        O("dve", lambda e: e.tensor_tensor(mout[0:1, 0:1], mout[0:1, 0:1], extra, ALU.mult),
          reads=[moutb, extrab], writes=[moutb])


def build_elementwise_launch(which):
    nc = bass.Bass("TRN2", target_bir_lowering=False)
    em = EM(nc)
    st = ExitStack()
    O, bf = em.op, em.bf
    with st:
        e = st.enter_context
        sb = lambda n, s, d: e(nc.sbuf_tensor(n, s, d))
        idf = sb("idf", [128, 128], F32)
        tmp1 = sb("tmp1", [128, 128], F32)
        ones1 = sb("ones1", [1, 128], F32)
        psT = e(nc.psum_tensor("psT", [128, 512], F32))
        mc = sb("mc", [128, 32], F32)
        mo = sb("mo", [1, 4], F32)
        c = bf("const")
        mcb = bf("mc"); psTb = bf("psT"); mob = bf("mo")
        O("dve", lambda e: e.memset(tmp1[:], 1.0), writes=[c])
        O("dve", lambda e: e.memset(ones1[:], 1.0), writes=[c])
        O("pool", lambda e: e.affine_select(idf[:], tmp1[:], [[-1, 128]], ALU.is_equal, 0.0,
                                            base=0, channel_multiplier=1), reads=[c], writes=[c])
        O("dve", lambda e: e.memset(mc[:], 0.0), writes=[mcb])
        ost = []

        def brow_col(dstcol, srcap, srcb):
            O("pe", lambda e: e.matmul(psT[:, 0:1], ones1[:], srcap, start=True, stop=True),
              reads=[srcb, c], writes=[psTb])
            O("dve", lambda e: e.tensor_copy(dstcol, psT[:, 0:1]), reads=[psTb], writes=[srcb])

        if which == 2:
            xh = nc.dram_tensor("xo", [SEG, D], F32, kind="ExternalInput").ap()
            oyh = nc.dram_tensor("oy", [SEG, D], F32, kind="ExternalInput").ap()
            onwh = nc.dram_tensor("onw", [1, D], F32, kind="ExternalInput").ap()
            onbh = nc.dram_tensor("onb", [1, D], F32, kind="ExternalInput").ap()
            x1h = nc.dram_tensor("x1", [SEG, D], F32, kind="ExternalOutput").ap()
            h2h = nc.dram_tensor("h2", [SEG, D], F32, kind="ExternalOutput").ap()
            mh = nc.dram_tensor("m1", [1, 1], F32, kind="ExternalOutput").ap()
            onw = sb("onws", [128, D], F32)
            onb = sb("onbs", [128, D], F32)
            epsl = sb("epsl", [128, 1], F32)
            mhf = sb("mhf", [128, 1], F32)
            O("dve", lambda e: e.memset(epsl[:], EPS_LN), writes=[c])
            O("dve", lambda e: e.memset(mhf[:], -0.5), writes=[c])
            O("sp", lambda e: e.dma_start(out=onw[0:1, :], in_=onwh), writes=[c], dma=True)
            O("sp", lambda e: e.dma_start(out=onb[0:1, :], in_=onbh), writes=[c], dma=True)
            for dst in (onw, onb):
                for half in range(2):
                    O("pe", lambda e, dst=dst, half=half: e.matmul(psT[:, 0:512], ones1[:],
                                                                   dst[0:1, 512 * half:512 * (half + 1)],
                                                                   start=True, stop=True),
                      reads=[c], writes=[psTb])
                    O("dve", lambda e, dst=dst, half=half: e.tensor_copy(dst[:, 512 * half:512 * (half + 1)],
                                                                         psT[:, 0:512]),
                      reads=[psTb], writes=[c])
            xt = [sb(f"xt{i}", [128, D], F32) for i in range(2)]
            ot = [sb(f"ot{i}", [128, D], F32) for i in range(2)]
            w1 = [sb(f"w1{i}", [128, D], F32) for i in range(2)]
            w2 = [sb(f"w2{i}", [128, D], F32) for i in range(2)]
            sq = [sb(f"sq{i}", [128, D], F32) for i in range(2)]
            stt = [sb(f"stt{i}", [128, 8], F32) for i in range(2)]
            for ci in range(NT_S):
                i = ci % 2
                xb_ = bf(f"x{i}"); ob_ = bf(f"o{i}"); w1b = bf(f"w1{i}"); w2b = bf(f"w2{i}")
                sqb = bf(f"sq{i}"); sttb = bf(f"stt{i}")
                O("sp", lambda e, ci=ci, i=i: e.dma_start(out=xt[i][:], in_=xh[T * ci:T * (ci + 1), :]),
                  writes=[xb_], dma=True)
                O("sp", lambda e, ci=ci, i=i: e.dma_start(out=ot[i][:], in_=oyh[T * ci:T * (ci + 1), :]),
                  writes=[ob_], dma=True)
                ln_stats(O, ot[i], sq[i], stt[i], ob_, sqb, sttb, epsl[:, 0:1], mhf[:, 0:1], c)
                O("dve", lambda e, i=i: e.tensor_scalar(w1[i][:], ot[i][:], stt[i][:, 5:6],
                                                        stt[i][:, 6:7], ALU.mult, ALU.add),
                  reads=[ob_, sttb], writes=[w1b])
                O("dve", lambda e, i=i: e.tensor_tensor(w1[i][:], w1[i][:], onw[:], ALU.mult),
                  reads=[w1b, c], writes=[w1b])
                O("dve", lambda e, i=i: e.tensor_tensor(w1[i][:], w1[i][:], onb[:], ALU.add),
                  reads=[w1b, c], writes=[w1b])
                O("dve", lambda e, i=i: e.tensor_tensor(w1[i][:], w1[i][:], xt[i][:], ALU.add),
                  reads=[w1b, xb_], writes=[w1b])
                ost.append(O("pool", lambda e, ci=ci, i=i: e.dma_start(out=x1h[T * ci:T * (ci + 1), :],
                                                                      in_=w1[i][:]),
                             reads=[w1b], dma=True))
                ln_stats(O, w1[i], sq[i], stt[i], w1b, sqb, sttb, epsl[:, 0:1], mhf[:, 0:1], c)
                O("dve", lambda e, i=i: e.tensor_scalar(w2[i][:], w1[i][:], stt[i][:, 5:6],
                                                        stt[i][:, 6:7], ALU.mult, ALU.add),
                  reads=[w1b, sttb], writes=[w2b])
                ost.append(O("pool", lambda e, ci=ci, i=i: e.dma_start(out=h2h[T * ci:T * (ci + 1), :],
                                                                      in_=w2[i][:]),
                             reads=[w2b], dma=True))
                O("dve", lambda e, ci=ci, i=i: e.tensor_reduce(mc[:, ci:ci + 1], w2[i][:], AX.X, ALU.max,
                                                               apply_absolute_value=True),
                  reads=[w2b], writes=[mcb])
            max_chain(O, mc, mcb, idf, psT, psTb, mo, mob, NT_S)
            ost.append(O("pool", lambda e: e.dma_start(out=mh, in_=mo[0:1, 0:1]), reads=[mob], dma=True))

        elif which == 4:
            gh = nc.dram_tensor("gr", [SEG, DI], F32, kind="ExternalInput").ap()
            mgh = nc.dram_tensor("mg8", [1, 8], F32, kind="ExternalInput").ap()
            m8h = nc.dram_tensor("m8", [1, 8], F32, kind="ExternalInput").ap()
            g2h = nc.dram_tensor("g2", [SEG, DI], F32, kind="ExternalOutput").ap()
            m2h = nc.dram_tensor("mg2", [1, 1], F32, kind="ExternalOutput").ap()
            sc = sb("sc", [1, 16], F32)
            scc = sb("scc", [128, 2], F32)
            scb = bf("sc")
            O("sp", lambda e: e.dma_start(out=sc[0:1, 0:8], in_=mgh), writes=[scb], dma=True)
            O("dve", lambda e: e.tensor_reduce(sc[0:1, 8:9], sc[0:1, 0:8], AX.X, ALU.max),
              reads=[scb], writes=[scb])
            O("sp", lambda e: e.dma_start(out=sc[0:1, 0:8], in_=m8h), writes=[scb], dma=True)
            O("dve", lambda e: e.tensor_reduce(sc[0:1, 12:13], sc[0:1, 0:8], AX.X, ALU.max),
              reads=[scb], writes=[scb])
            O("dve", lambda e: e.tensor_scalar(sc[0:1, 13:14], sc[0:1, 12:13], 1.0 / 32767.0, 1e-12,
                                               ALU.mult, ALU.max), reads=[scb], writes=[scb])
            O("dve", lambda e: e.tensor_scalar(sc[0:1, 9:10], sc[0:1, 8:9], 1.0 / 32767.0, 1e-12,
                                               ALU.mult, ALU.max), reads=[scb], writes=[scb])
            O("dve", lambda e: e.reciprocal(sc[0:1, 10:11], sc[0:1, 9:10]), reads=[scb], writes=[scb])
            O("dve", lambda e: e.tensor_tensor(sc[0:1, 14:15], sc[0:1, 13:14], sc[0:1, 10:11], ALU.mult),
              reads=[scb], writes=[scb])
            O("dve", lambda e: e.tensor_scalar(sc[0:1, 11:12], sc[0:1, 9:10], 1.0 / 65536.0, None, ALU.mult),
              reads=[scb], writes=[scb])
            brow_col(scc[:, 0:1], sc[0:1, 14:15], scb)
            brow_col(scc[:, 1:2], sc[0:1, 9:10], scb)
            gt = [sb(f"gt{i}", [128, DI], F32) for i in range(2)]
            gi = [sb(f"gi{i}", [128, DI], I16) for i in range(2)]
            gq = [sb(f"gq{i}", [128, DI], F32) for i in range(2)]
            sg = [sb(f"sg{i}", [128, DI], F32) for i in range(2)]
            su = [sb(f"su{i}", [128, DI], U16) for i in range(2)]
            sf = [sb(f"sf{i}", [128, DI], F32) for i in range(2)]
            for ci in range(NT_S):
                i = ci % 2
                gb = bf(f"g{i}")
                O("sp", lambda e, ci=ci, i=i: e.dma_start(out=gt[i][:], in_=gh[T * ci:T * (ci + 1), :]),
                  writes=[gb], dma=True)
                O("dve", lambda e, i=i: e.tensor_scalar(gi[i][:], gt[i][:], scc[:, 0:1], None, ALU.mult),
                  reads=[gb, scb], writes=[gb])
                O("dve", lambda e, i=i: e.tensor_copy(gq[i][:], gi[i][:]), reads=[gb], writes=[gb])
                O("act", lambda e, i=i: e.activation(sg[i][:], gq[i][:], AF.Sigmoid, scale=scc[:, 1:2]),
                  reads=[gb, scb], writes=[gb])
                O("dve", lambda e, i=i: e.tensor_scalar(su[i][:], sg[i][:], 65536.0, None, ALU.mult),
                  reads=[gb], writes=[gb])
                O("pool", lambda e, i=i: e.tensor_copy(sf[i][:], su[i][:]), reads=[gb], writes=[gb])
                O("dve", lambda e, i=i: e.tensor_tensor(sg[i][:], gq[i][:], sf[i][:], ALU.mult),
                  reads=[gb], writes=[gb])
                ost.append(O("pool", lambda e, ci=ci, i=i: e.dma_start(out=g2h[T * ci:T * (ci + 1), :],
                                                                      in_=sg[i][:]),
                             reads=[gb], dma=True))
                O("dve", lambda e, ci=ci, i=i: e.tensor_reduce(mc[:, ci:ci + 1], sg[i][:], AX.X, ALU.max,
                                                               apply_absolute_value=True),
                  reads=[gb], writes=[mcb])
            max_chain(O, mc, mcb, idf, psT, psTb, mo, mob, NT_S, extra=sc[0:1, 11:12], extrab=scb)
            ost.append(O("pool", lambda e: e.dma_start(out=m2h, in_=mo[0:1, 0:1]), reads=[mob], dma=True))

        elif which == 5:
            uh = nc.dram_tensor("ur", [SEG, DI], F32, kind="ExternalInput").ap()
            g2h = nc.dram_tensor("g2", [SEG, DI], F32, kind="ExternalInput").ap()
            muh = nc.dram_tensor("mu8", [1, 8], F32, kind="ExternalInput").ap()
            mg2h = nc.dram_tensor("mg28", [1, 8], F32, kind="ExternalInput").ap()
            mgh = nc.dram_tensor("mg8", [1, 8], F32, kind="ExternalInput").ap()
            m8h = nc.dram_tensor("m8", [1, 8], F32, kind="ExternalInput").ap()
            oph = nc.dram_tensor("op", [SEG, DI], F32, kind="ExternalOutput").ap()
            moh = nc.dram_tensor("mo1", [1, 1], F32, kind="ExternalOutput").ap()
            sc = sb("sc", [1, 32], F32)
            scc = sb("scc", [128, 2], F32)
            scb = bf("sc")
            O("sp", lambda e: e.dma_start(out=sc[0:1, 0:8], in_=muh), writes=[scb], dma=True)
            O("dve", lambda e: e.tensor_reduce(sc[0:1, 16:17], sc[0:1, 0:8], AX.X, ALU.max),
              reads=[scb], writes=[scb])
            O("sp", lambda e: e.dma_start(out=sc[0:1, 0:8], in_=mg2h), writes=[scb], dma=True)
            O("dve", lambda e: e.tensor_reduce(sc[0:1, 17:18], sc[0:1, 0:8], AX.X, ALU.max),
              reads=[scb], writes=[scb])
            O("sp", lambda e: e.dma_start(out=sc[0:1, 0:8], in_=mgh), writes=[scb], dma=True)
            O("dve", lambda e: e.tensor_reduce(sc[0:1, 18:19], sc[0:1, 0:8], AX.X, ALU.max),
              reads=[scb], writes=[scb])
            O("dve", lambda e: e.tensor_scalar(sc[0:1, 19:20], sc[0:1, 16:17], 1.0 / 32767.0, 1e-12,
                                               ALU.mult, ALU.max), reads=[scb], writes=[scb])
            O("dve", lambda e: e.reciprocal(sc[0:1, 20:21], sc[0:1, 19:20]), reads=[scb], writes=[scb])
            O("dve", lambda e: e.tensor_scalar(sc[0:1, 21:22], sc[0:1, 17:18], 1.0 / 32767.0, 1e-12,
                                               ALU.mult, ALU.max), reads=[scb], writes=[scb])
            O("dve", lambda e: e.reciprocal(sc[0:1, 22:23], sc[0:1, 21:22]), reads=[scb], writes=[scb])
            O("dve", lambda e: e.tensor_scalar(sc[0:1, 23:24], sc[0:1, 18:19], 1.0 / 32767.0, 1e-12,
                                               ALU.mult, ALU.max), reads=[scb], writes=[scb])
            O("dve", lambda e: e.tensor_scalar(sc[0:1, 24:25], sc[0:1, 23:24], 1.0 / 65536.0, None, ALU.mult),
              reads=[scb], writes=[scb])
            O("dve", lambda e: e.tensor_tensor(sc[0:1, 24:25], sc[0:1, 24:25], sc[0:1, 22:23], ALU.mult),
              reads=[scb], writes=[scb])
            O("dve", lambda e: e.tensor_tensor(sc[0:1, 25:26], sc[0:1, 19:20], sc[0:1, 21:22], ALU.mult),
              reads=[scb], writes=[scb])
            O("sp", lambda e: e.dma_start(out=sc[0:1, 0:8], in_=m8h), writes=[scb], dma=True)
            O("dve", lambda e: e.tensor_reduce(sc[0:1, 26:27], sc[0:1, 0:8], AX.X, ALU.max),
              reads=[scb], writes=[scb])
            O("dve", lambda e: e.tensor_scalar(sc[0:1, 27:28], sc[0:1, 26:27], 1.0 / 32767.0, 1e-12,
                                               ALU.mult, ALU.max), reads=[scb], writes=[scb])
            O("dve", lambda e: e.tensor_tensor(sc[0:1, 28:29], sc[0:1, 27:28], sc[0:1, 20:21], ALU.mult),
              reads=[scb], writes=[scb])
            brow_col(scc[:, 0:1], sc[0:1, 28:29], scb)
            brow_col(scc[:, 1:2], sc[0:1, 24:25], scb)
            ut = [sb(f"ut{i}", [128, DI], F32) for i in range(2)]
            g2t = [sb(f"g2t{i}", [128, DI], F32) for i in range(2)]
            ui = [sb(f"ui{i}", [128, DI], I16) for i in range(2)]
            uf = [sb(f"uf{i}", [128, DI], F32) for i in range(2)]
            g2i = [sb(f"g2i{i}", [128, DI], I16) for i in range(2)]
            g2f = [sb(f"g2f{i}", [128, DI], F32) for i in range(2)]
            for ci in range(NT_S):
                i = ci % 2
                ub = bf(f"u{i}"); gb = bf(f"g{i}")
                O("sp", lambda e, ci=ci, i=i: e.dma_start(out=ut[i][:], in_=uh[T * ci:T * (ci + 1), :]),
                  writes=[ub], dma=True)
                O("sp", lambda e, ci=ci, i=i: e.dma_start(out=g2t[i][:], in_=g2h[T * ci:T * (ci + 1), :]),
                  writes=[gb], dma=True)
                O("dve", lambda e, i=i: e.tensor_scalar(ui[i][:], ut[i][:], scc[:, 0:1], None, ALU.mult),
                  reads=[ub, scb], writes=[ub])
                O("pool", lambda e, i=i: e.tensor_copy(uf[i][:], ui[i][:]), reads=[ub], writes=[ub])
                O("dve", lambda e, i=i: e.tensor_scalar(g2i[i][:], g2t[i][:], scc[:, 1:2], None, ALU.mult),
                  reads=[gb, scb], writes=[gb])
                O("pool", lambda e, i=i: e.tensor_copy(g2f[i][:], g2i[i][:]), reads=[gb], writes=[gb])
                O("dve", lambda e, i=i: e.tensor_tensor(uf[i][:], uf[i][:], g2f[i][:], ALU.mult),
                  reads=[ub, gb], writes=[ub])
                ost.append(O("pool", lambda e, ci=ci, i=i: e.dma_start(out=oph[T * ci:T * (ci + 1), :],
                                                                      in_=uf[i][:]),
                             reads=[ub], dma=True))
                O("dve", lambda e, ci=ci, i=i: e.tensor_reduce(mc[:, ci:ci + 1], uf[i][:], AX.X, ALU.max,
                                                               apply_absolute_value=True),
                  reads=[ub], writes=[mcb])
            max_chain(O, mc, mcb, idf, psT, psTb, mo, mob, NT_S, extra=sc[0:1, 25:26], extrab=scb)
            ost.append(O("pool", lambda e: e.dma_start(out=moh, in_=mo[0:1, 0:1]), reads=[mob], dma=True))

        em.finish(ost)
        em.build(st)
    return nc


def build_L3():
    nc = bass.Bass("TRN2", target_bir_lowering=False)
    h2h = nc.dram_tensor("h2", [SEG, D], F32, kind="ExternalInput").ap()
    m8h = nc.dram_tensor("m8", [1, 8], F32, kind="ExternalInput").ap()
    wgth = nc.dram_tensor("wgt", [D, DI], BF16, kind="ExternalInput").ap()
    wuth = nc.dram_tensor("wut", [D, DI], BF16, kind="ExternalInput").ap()
    urh = nc.dram_tensor("ur", [SEG, DI], F32, kind="ExternalOutput").ap()
    grh = nc.dram_tensor("gr", [SEG, DI], F32, kind="ExternalOutput").ap()
    muh = nc.dram_tensor("mu1", [1, 1], F32, kind="ExternalOutput").ap()
    mgh = nc.dram_tensor("mg1", [1, 1], F32, kind="ExternalOutput").ap()
    em = EM(nc)
    st = ExitStack()
    O, bf = em.op, em.bf
    with st:
        e = st.enter_context
        sb = lambda n, s, d: e(nc.sbuf_tensor(n, s, d))
        idf = sb("idf", [128, 128], F32)
        tmp1 = sb("tmp1", [128, 128], F32)
        ones1 = sb("ones1", [1, 128], F32)
        psT = e(nc.psum_tensor("psT", [128, 128], F32))
        psM = [e(nc.psum_tensor(f"psM{i}", [128, 512], F32)) for i in range(2)]
        mcu = sb("mcu", [128, 32], F32)
        mcg = sb("mcg", [128, 32], F32)
        mo = sb("mo", [1, 4], F32)
        sc = sb("sc", [1, 16], F32)
        rcol = sb("rcol", [128, 1], F32)
        wgt = sb("wgts", [128, 8 * DI], BF16)
        wut = sb("wuts", [128, 8 * DI], BF16)
        c = bf("const"); psTb = bf("psT"); scb = bf("sc")
        mcub = bf("mcu"); mcgb = bf("mcg"); mob = bf("mo")
        O("dve", lambda e: e.memset(tmp1[:], 1.0), writes=[c])
        O("dve", lambda e: e.memset(ones1[:], 1.0), writes=[c])
        O("pool", lambda e: e.affine_select(idf[:], tmp1[:], [[-1, 128]], ALU.is_equal, 0.0,
                                            base=0, channel_multiplier=1), reads=[c], writes=[c])
        O("dve", lambda e: e.memset(mcu[:], 0.0), writes=[mcub])
        O("dve", lambda e: e.memset(mcg[:], 0.0), writes=[mcgb])
        O("sp", lambda e: e.dma_start(out=sc[0:1, 0:8], in_=m8h), writes=[scb], dma=True)
        O("dve", lambda e: e.tensor_reduce(sc[0:1, 8:9], sc[0:1, 0:8], AX.X, ALU.max), reads=[scb], writes=[scb])
        O("dve", lambda e: e.tensor_scalar(sc[0:1, 9:10], sc[0:1, 8:9], 1.0 / 32767.0, 1e-12, ALU.mult, ALU.max),
          reads=[scb], writes=[scb])
        O("dve", lambda e: e.reciprocal(sc[0:1, 10:11], sc[0:1, 9:10]), reads=[scb], writes=[scb])
        O("pe", lambda e: e.matmul(psT[:, 0:1], ones1[:], sc[0:1, 10:11], start=True, stop=True),
          reads=[scb, c], writes=[psTb])
        O("dve", lambda e: e.tensor_copy(rcol[:], psT[:, 0:1]), reads=[psTb], writes=[c])
        wgb = bf("wgt"); wub = bf("wut")
        for ck in range(8):
            O("sp", lambda e, ck=ck: e.dma_start(out=wgt[:, DI * ck:DI * (ck + 1)],
                                                 in_=wgth[128 * ck:128 * (ck + 1), :]), writes=[wgb], dma=True)
            O("act", lambda e, ck=ck: e.dma_start(out=wut[:, DI * ck:DI * (ck + 1)],
                                                  in_=wuth[128 * ck:128 * (ck + 1), :]), writes=[wub], dma=True)
        xt = [sb(f"xt{i}", [128, D], F32) for i in range(2)]
        xi = [sb(f"xi{i}", [128, D], I16) for i in range(2)]
        xb = [sb(f"xb{i}", [128, D], BF16) for i in range(2)]
        xf = [sb(f"xf{i}", [128, D], BF16) for i in range(2)]
        rw = [sb(f"rw{i}", [128, DI], F32) for i in range(4)]
        ost = []
        for ci in range(NT_S):
            i = ci % 2
            xbb = bf(f"x{i}"); xfb = bf(f"xf{i}")
            O("sp", lambda e, ci=ci, i=i: e.dma_start(out=xt[i][:], in_=h2h[T * ci:T * (ci + 1), :]),
              writes=[xbb], dma=True)
            O("dve", lambda e, i=i: e.tensor_scalar(xi[i][:], xt[i][:], rcol[:, 0:1], None, ALU.mult),
              reads=[xbb, c], writes=[xbb])
            O("dve", lambda e, i=i: e.tensor_copy(xb[i][:], xi[i][:]), reads=[xbb], writes=[xbb])
            for ck in range(8):
                q = "sp" if ck % 2 == 0 else "act"
                O(q, lambda e, ck=ck, i=i: e.dma_start(out=xf[i][:, 128 * ck:128 * (ck + 1)],
                                                       in_=xb[i][:, 128 * ck:128 * (ck + 1)], transpose=True),
                  reads=[xbb], writes=[xfb], dma=True)
            for wres, wbuf, rwb0, outh, mctile, mcb2 in ((wut, bf("wut"), 0, urh, mcu, mcub),
                                                         (wgt, bf("wgt"), 2, grh, mcg, mcgb)):
                rwt = rw[rwb0 + i]
                rb = bf(f"rw{rwb0 + i}")
                for oc in range(6):
                    pj = oc % 2
                    pb2 = bf(f"psM{pj}")
                    for ck in range(8):
                        O("pe", lambda e, ck=ck, oc=oc, wres=wres, pj=pj, i=i: e.matmul(
                            psM[pj][:], xf[i][:, 128 * ck:128 * (ck + 1)],
                            wres[:, DI * ck + 512 * oc:DI * ck + 512 * (oc + 1)],
                            start=(ck == 0), stop=(ck == 7)),
                          reads=[xfb, wbuf], writes=[pb2])
                    O("act", lambda e, oc=oc, rwt=rwt, pj=pj: e.copy(rwt[:, 512 * oc:512 * (oc + 1)], psM[pj][:]),
                      reads=[pb2], writes=[rb])
                ost.append(O("pool", lambda e, ci=ci, outh=outh, rwt=rwt: e.dma_start(
                    out=outh[T * ci:T * (ci + 1), :], in_=rwt[:]), reads=[rb], dma=True))
                O("dve", lambda e, ci=ci, rwt=rwt, mctile=mctile: e.tensor_reduce(
                    mctile[:, ci:ci + 1], rwt[:], AX.X, ALU.max, apply_absolute_value=True),
                  reads=[rb], writes=[mcb2])
        max_chain(O, mcu, mcub, idf, psT, psTb, mo, mob, NT_S, extra=sc[0:1, 9:10], extrab=scb)
        ost.append(O("pool", lambda e: e.dma_start(out=muh, in_=mo[0:1, 0:1]), reads=[mob], dma=True))
        O("dve", lambda e: e.tensor_copy(mcu[:, 0:NT_S], mcg[:, 0:NT_S]), reads=[mcgb], writes=[mcub])
        max_chain(O, mcu, mcub, idf, psT, psTb, mo, mob, NT_S, extra=sc[0:1, 9:10], extrab=scb)
        ost.append(O("pool", lambda e: e.dma_start(out=mgh, in_=mo[0:1, 0:1]), reads=[mob], dma=True))
        em.finish(ost)
        em.build(st)
    return nc


def build_L6():
    nc = bass.Bass("TRN2", target_bir_lowering=False)
    oph = nc.dram_tensor("op", [SEG, DI], F32, kind="ExternalInput").ap()
    x1h = nc.dram_tensor("x1", [SEG, D], F32, kind="ExternalInput").ap()
    muh = nc.dram_tensor("mu8", [1, 8], F32, kind="ExternalInput").ap()
    mg2h = nc.dram_tensor("mg28", [1, 8], F32, kind="ExternalInput").ap()
    moh = nc.dram_tensor("mo8", [1, 8], F32, kind="ExternalInput").ap()
    wdth = nc.dram_tensor("wdt", [DI, D], BF16, kind="ExternalInput").ap()
    yh = nc.dram_tensor("y", [SEG, D], F32, kind="ExternalOutput").ap()
    em = EM(nc)
    st = ExitStack()
    O, bf = em.op, em.bf
    with st:
        e = st.enter_context
        sb = lambda n, s, d: e(nc.sbuf_tensor(n, s, d))
        ones1 = sb("ones1", [1, 128], F32)
        psT = e(nc.psum_tensor("psT", [128, 128], F32))
        psM = [e(nc.psum_tensor(f"psM{i}", [128, 512], F32)) for i in range(2)]
        sc = sb("sc", [1, 32], F32)
        scc = sb("scc", [128, 2], F32)
        wdt = sb("wdts", [128, 24 * D], BF16)
        c = bf("const"); psTb = bf("psT"); scb = bf("sc"); wdb = bf("wdt")
        O("dve", lambda e: e.memset(ones1[:], 1.0), writes=[c])
        O("sp", lambda e: e.dma_start(out=sc[0:1, 0:8], in_=muh), writes=[scb], dma=True)
        O("dve", lambda e: e.tensor_reduce(sc[0:1, 16:17], sc[0:1, 0:8], AX.X, ALU.max), reads=[scb], writes=[scb])
        O("sp", lambda e: e.dma_start(out=sc[0:1, 0:8], in_=mg2h), writes=[scb], dma=True)
        O("dve", lambda e: e.tensor_reduce(sc[0:1, 17:18], sc[0:1, 0:8], AX.X, ALU.max), reads=[scb], writes=[scb])
        O("sp", lambda e: e.dma_start(out=sc[0:1, 0:8], in_=moh), writes=[scb], dma=True)
        O("dve", lambda e: e.tensor_reduce(sc[0:1, 18:19], sc[0:1, 0:8], AX.X, ALU.max), reads=[scb], writes=[scb])
        O("dve", lambda e: e.tensor_scalar(sc[0:1, 19:20], sc[0:1, 16:17], 1.0 / 32767.0, 1e-12,
                                           ALU.mult, ALU.max), reads=[scb], writes=[scb])
        O("dve", lambda e: e.tensor_scalar(sc[0:1, 20:21], sc[0:1, 17:18], 1.0 / 32767.0, 1e-12,
                                           ALU.mult, ALU.max), reads=[scb], writes=[scb])
        O("dve", lambda e: e.tensor_scalar(sc[0:1, 21:22], sc[0:1, 18:19], 1.0 / 32767.0, 1e-12,
                                           ALU.mult, ALU.max), reads=[scb], writes=[scb])
        O("dve", lambda e: e.reciprocal(sc[0:1, 22:23], sc[0:1, 21:22]), reads=[scb], writes=[scb])
        O("dve", lambda e: e.tensor_tensor(sc[0:1, 23:24], sc[0:1, 19:20], sc[0:1, 20:21], ALU.mult),
          reads=[scb], writes=[scb])
        O("dve", lambda e: e.tensor_tensor(sc[0:1, 23:24], sc[0:1, 23:24], sc[0:1, 22:23], ALU.mult),
          reads=[scb], writes=[scb])
        for colsrc, coldst in ((23, 0), (21, 1)):
            O("pe", lambda e, colsrc=colsrc: e.matmul(psT[:, 0:1], ones1[:], sc[0:1, colsrc:colsrc + 1],
                                                      start=True, stop=True), reads=[scb, c], writes=[psTb])
            O("dve", lambda e, coldst=coldst: e.tensor_copy(scc[:, coldst:coldst + 1], psT[:, 0:1]),
              reads=[psTb], writes=[scb])
        for ck in range(24):
            q = "sp" if ck % 2 == 0 else "act"
            O(q, lambda e, ck=ck: e.dma_start(out=wdt[:, D * ck:D * (ck + 1)],
                                              in_=wdth[128 * ck:128 * (ck + 1), :]), writes=[wdb], dma=True)
        ot = [sb(f"ot{i}", [128, DI], F32) for i in range(2)]
        oi = [sb(f"oi{i}", [128, DI], I16) for i in range(2)]
        ob = [sb(f"ob{i}", [128, DI], BF16) for i in range(2)]
        of = [sb(f"of{i}", [128, DI], BF16) for i in range(2)]
        x1t = [sb(f"x1t{i}", [128, D], F32) for i in range(2)]
        yt = [sb(f"yt{i}", [128, D], F32) for i in range(2)]
        ost = []
        for ci in range(NT_S):
            i = ci % 2
            obf = bf(f"o{i}"); off = bf(f"of{i}"); x1b = bf(f"x1{i}"); yb = bf(f"y{i}")
            O("sp", lambda e, ci=ci, i=i: e.dma_start(out=ot[i][:], in_=oph[T * ci:T * (ci + 1), :]),
              writes=[obf], dma=True)
            O("sp", lambda e, ci=ci, i=i: e.dma_start(out=x1t[i][:], in_=x1h[T * ci:T * (ci + 1), :]),
              writes=[x1b], dma=True)
            O("dve", lambda e, i=i: e.tensor_scalar(oi[i][:], ot[i][:], scc[:, 0:1], None, ALU.mult),
              reads=[obf, scb], writes=[obf])
            O("dve", lambda e, i=i: e.tensor_copy(ob[i][:], oi[i][:]), reads=[obf], writes=[obf])
            for ck in range(24):
                q = "sp" if ck % 2 == 0 else "act"
                O(q, lambda e, ck=ck, i=i: e.dma_start(out=of[i][:, 128 * ck:128 * (ck + 1)],
                                                       in_=ob[i][:, 128 * ck:128 * (ck + 1)], transpose=True),
                  reads=[obf], writes=[off], dma=True)
            for oc in range(2):
                pb2 = bf(f"psM{oc}")
                for ck in range(24):
                    O("pe", lambda e, ck=ck, oc=oc, i=i: e.matmul(
                        psM[oc][:], of[i][:, 128 * ck:128 * (ck + 1)],
                        wdt[:, D * ck + 512 * oc:D * ck + 512 * (oc + 1)],
                        start=(ck == 0), stop=(ck == 23)),
                      reads=[off, wdb], writes=[pb2])
                O("dve", lambda e, oc=oc, i=i: e.scalar_tensor_tensor(
                    yt[i][:, 512 * oc:512 * (oc + 1)], psM[oc][:], scc[:, 1:2],
                    x1t[i][:, 512 * oc:512 * (oc + 1)], ALU.mult, ALU.add),
                  reads=[bf(f"psM{oc}"), x1b, scb], writes=[yb])
            ost.append(O("pool", lambda e, ci=ci, i=i: e.dma_start(out=yh[T * ci:T * (ci + 1), :],
                                                                  in_=yt[i][:]),
                         reads=[yb], dma=True))
        em.finish(ost)
        em.build(st)
    return nc


_cache = {}

def _get(name, builder):
    if name not in _cache:
        _cache[name] = builder()
    return _cache[name]


def kernel(L0_attn_x, L0_attn_W, L0_attn_k_norm_w, L0_attn_o_norm_w, L0_attn_o_norm_b,
           L0_attn_eta_raw, L0_attn_lam_logit, L0_ffn_w_gate, L0_ffn_w_up, L0_ffn_w_down):
    x = np.ascontiguousarray(np.asarray(L0_attn_x, dtype=np.float32))
    W = np.ascontiguousarray(np.asarray(L0_attn_W, dtype=np.float32))
    knw = np.ascontiguousarray(np.asarray(L0_attn_k_norm_w, dtype=np.float32)).reshape(1, DH)
    onw = np.ascontiguousarray(np.asarray(L0_attn_o_norm_w, dtype=np.float32)).reshape(1, D)
    onb = np.ascontiguousarray(np.asarray(L0_attn_o_norm_b, dtype=np.float32)).reshape(1, D)
    eta = np.ascontiguousarray(np.asarray(L0_attn_eta_raw, dtype=np.float32))
    lam = np.ascontiguousarray(np.asarray(L0_attn_lam_logit, dtype=np.float32))
    wg = np.ascontiguousarray(np.asarray(L0_ffn_w_gate, dtype=np.float32))
    wu = np.ascontiguousarray(np.asarray(L0_ffn_w_up, dtype=np.float32))
    wd = np.ascontiguousarray(np.asarray(L0_ffn_w_down, dtype=np.float32))
    cores = list(range(NCORE))

    in1 = []
    for j in cores:
        b, hg = j // 4, j % 4
        perm = np.roll(np.arange(D), -256 * hg)
        in1.append({
            "x": np.ascontiguousarray(x[b][:, perm]),
            "w": np.ascontiguousarray(W[256 * hg:256 * (hg + 1), :][:, perm]),
            "knw": knw,
            "lam4": np.ascontiguousarray(lam[4 * hg:4 * (hg + 1)]).reshape(1, 4),
            "eta4": np.ascontiguousarray(eta[4 * hg:4 * (hg + 1)]).reshape(1, 4),
            "wg": np.ascontiguousarray(wg[384 * j:384 * (j + 1), :]),
            "wu": np.ascontiguousarray(wu[384 * j:384 * (j + 1), :]),
            "wd": np.ascontiguousarray(wd[128 * j:128 * (j + 1), :]),
        })
    r1 = run_bass_kernel_spmd(_get("L1", build_L1), in1, core_ids=cores).results
    wgT = np.ascontiguousarray(np.concatenate([np.asarray(r1[j]["wgt"]) for j in cores], axis=1))
    wuT = np.ascontiguousarray(np.concatenate([np.asarray(r1[j]["wut"]) for j in cores], axis=1))
    wdT = np.ascontiguousarray(np.concatenate([np.asarray(r1[j]["wdt"]) for j in cores], axis=1))

    in2 = []
    for j in cores:
        b, s = j // 4, j % 4
        oy = np.concatenate([np.asarray(r1[4 * b + hg]["od"])[SEG * s:SEG * (s + 1), :] for hg in range(4)],
                            axis=1)
        in2.append({"xo": np.ascontiguousarray(x[b][SEG * s:SEG * (s + 1), :]),
                    "oy": np.ascontiguousarray(oy), "onw": onw, "onb": onb})
    r2 = run_bass_kernel_spmd(_get("L2", lambda: build_elementwise_launch(2)), in2, core_ids=cores).results
    m8 = np.ascontiguousarray(np.concatenate([np.asarray(r2[j]["m1"]).reshape(1, 1) for j in cores], axis=1))

    in3 = [{"h2": np.asarray(r2[j]["h2"]), "m8": m8, "wgt": wgT, "wut": wuT} for j in cores]
    r3 = run_bass_kernel_spmd(_get("L3", build_L3), in3, core_ids=cores).results
    mu8 = np.ascontiguousarray(np.concatenate([np.asarray(r3[j]["mu1"]).reshape(1, 1) for j in cores], axis=1))
    mg8 = np.ascontiguousarray(np.concatenate([np.asarray(r3[j]["mg1"]).reshape(1, 1) for j in cores], axis=1))

    in4 = [{"gr": np.asarray(r3[j]["gr"]), "mg8": mg8, "m8": m8} for j in cores]
    r4 = run_bass_kernel_spmd(_get("L4", lambda: build_elementwise_launch(4)), in4, core_ids=cores).results
    mg28 = np.ascontiguousarray(np.concatenate([np.asarray(r4[j]["mg2"]).reshape(1, 1) for j in cores], axis=1))

    in5 = [{"ur": np.asarray(r3[j]["ur"]), "g2": np.asarray(r4[j]["g2"]),
            "mu8": mu8, "mg28": mg28, "mg8": mg8, "m8": m8} for j in cores]
    r5 = run_bass_kernel_spmd(_get("L5", lambda: build_elementwise_launch(5)), in5, core_ids=cores).results
    mo8 = np.ascontiguousarray(np.concatenate([np.asarray(r5[j]["mo1"]).reshape(1, 1) for j in cores], axis=1))

    in6 = [{"op": np.asarray(r5[j]["op"]), "x1": np.asarray(r2[j]["x1"]),
            "mu8": mu8, "mg28": mg28, "mo8": mo8, "wdt": wdT} for j in cores]
    r6 = run_bass_kernel_spmd(_get("L6", build_L6), in6, core_ids=cores).results

    out = np.empty((B, LQ, D), dtype=np.float32)
    for j in cores:
        b, s = j // 4, j % 4
        out[b, SEG * s:SEG * (s + 1), :] = np.asarray(r6[j]["y"])
    return out
